# version 22
# baseline (speedup 1.0000x reference)
from contextlib import ExitStack

import numpy as np

import concourse.bass as bass
import concourse.mybir as mybir
from concourse.bass_utils import run_bass_kernel_spmd

F32 = mybir.dt.float32
BF16 = mybir.dt.bfloat16
AF = mybir.ActivationFunctionType
ALU = mybir.AluOpType
AX = mybir.AxisListType

NCORES = 8
TG = 512
NCH = 32
QCH = 8
NPASS_FULL = 4

SEM_INC = {"pe": 1, "act": 1, "dve": 1,
           "qcs": 16, "qcg": 16, "qx0": 16, "qx1": 16, "qx2": 16, "qx3": 16, "qo0": 16, "qo1": 16, "qo2": 16,
           "qwq0": 16, "qwq1": 16, "qbw0": 16, "qbw1": 16, "qbB0": 16, "qbB1": 16, "qbB2": 16, "qbB3": 16,
           "qwo0": 16, "qwo1": 16, "qwo2": 16, "qps0": 16, "qps1": 16, "qsc0": 16, "qsc1": 16,
           "hwq0": 16, "hwq1": 16, "hbw0": 16, "hbw1": 16, "hbB0": 16, "hbB1": 16, "hbB2": 16, "hbB3": 16}
STREAMS = ("pe", "act", "dve", "gp", "sp")


class Op:
    __slots__ = ("waits", "emit", "sem", "inc")

    def __init__(self, waits, emit, sem, inc):
        self.waits, self.emit, self.sem, self.inc = waits, emit, sem, inc


class Prog:

    def __init__(self):
        self.ops = {s: [] for s in STREAMS}
        self.cnt = {s: 0 for s in SEM_INC}
        self.knows = {s: {} for s in STREAMS}
        self.res = {}

    def op(self, stream, sem, emit, reads=(), writes=()):
        needs = {}

        def need(e, c):
            if c > needs.get(e, 0):
                needs[e] = c

        for k in reads:
            st = self.res.get(k)
            if st and st[0]:
                need(*st[0])
        for k in writes:
            st = self.res.get(k)
            if st:
                if st[0]:
                    need(*st[0])
                for e, c in st[1].items():
                    need(e, c)
        if SEM_INC[sem] == 16 and self.cnt[sem] > 0:
            need(sem, self.cnt[sem])
        waits = []
        for e, c in sorted(needs.items()):
            if e == sem and e == "pe":
                continue
            if self.knows[stream].get(e, 0) >= c:
                continue
            waits.append((e, c))
            self.knows[stream][e] = c
        self.cnt[sem] += SEM_INC[sem]
        after = (sem, self.cnt[sem])
        for k in reads:
            st = self.res.setdefault(k, [None, {}])
            st[1][sem] = after[1]
        for k in writes:
            self.res[k] = [after, {}]
        self.ops[stream].append(Op(waits, emit, sem, SEM_INC[sem]))
        return after

    def wait(self, stream, sem, value):
        self.ops[stream].append(Op([(sem, value)], None, None, 0))

    def run_stream(self, stream, eng, sems):
        for o in self.ops[stream]:
            for s, v in o.waits:
                eng.wait_ge(sems[s], v)
            if o.emit is not None:
                inst = o.emit(eng)
                inst.then_inc(sems[o.sem], o.inc)


def build_nc(npass):
    nc = bass.Bass("TRN2", target_bir_lowering=False)

    x_img = nc.dram_tensor("x_img", [npass, 128, NCH, TG], F32, kind="ExternalInput")
    wq_img = nc.dram_tensor("wq_img", [QCH, 128, NCH, 128], F32, kind="ExternalInput")
    base_img = nc.dram_tensor("base_img", [NCH, 128, NCH, 128], F32, kind="ExternalInput")
    bB_img = nc.dram_tensor("bB_img", [NCH, 128, QCH, 128], F32, kind="ExternalInput")
    wo2_img = nc.dram_tensor("wo2_img", [NCH, 128, NCH, 128], F32, kind="ExternalInput")
    at_img = nc.dram_tensor("at_img", [128, NCH, 64], F32, kind="ExternalInput")
    rw_img = nc.dram_tensor("rw_img", [128, NCH, 16], F32, kind="ExternalInput")
    m_img = nc.dram_tensor("m_img", [128, QCH, 128], F32, kind="ExternalInput")
    z_img = nc.dram_tensor("z_img", [128, QCH], F32, kind="ExternalInput")
    bq_img = nc.dram_tensor("bq_img", [128, QCH], F32, kind="ExternalInput")
    bb_img = nc.dram_tensor("bb_img", [128, NCH], F32, kind="ExternalInput")
    esel_img = nc.dram_tensor("esel_img", [16, QCH, 128], F32, kind="ExternalInput")
    eye_img = nc.dram_tensor("eye_img", [128, 128], F32, kind="ExternalInput")
    out_img = nc.dram_tensor("out_img", [npass, NCH, 128, TG], F32, kind="ExternalOutput")

    woe_s = nc.dram_tensor("woe_s", [NCH, 128, QCH, 128], BF16, kind="Internal")
    wq_s = nc.dram_tensor("wq_s", [QCH, 128, NCH, 128], BF16, kind="Internal")
    base_s = nc.dram_tensor("base_s", [NCH, 128, NCH, 128], BF16, kind="Internal")
    bB_s = nc.dram_tensor("bB_s", [NCH, 128, QCH, 128], BF16, kind="Internal")

    P = Prog()

    with ExitStack() as ctx:
        E = ctx.enter_context
        x_sb = E(nc.sbuf_tensor("x_sb", [128, NCH, TG], F32))
        xb = E(nc.sbuf_tensor("xb", [128, NCH, TG], BF16))
        sig = E(nc.sbuf_tensor("sig", [128, QCH, TG], BF16))
        retr = E(nc.sbuf_tensor("retr", [128, QCH, TG], BF16))
        xsq = E(nc.sbuf_tensor("xsq", [128, 2, TG], BF16))
        tmp0 = E(nc.sbuf_tensor("tmp0", [128, TG], F32))
        tmp1 = E(nc.sbuf_tensor("tmp1", [128, TG], F32))
        outb = E(nc.sbuf_tensor("outb", [128, 3, TG], F32))
        rows = E(nc.sbuf_tensor("rows", [1, 4 * TG], F32))
        wqb = E(nc.sbuf_tensor("wqb", [128, 2, NCH, 128], BF16))
        bwb = E(nc.sbuf_tensor("bwb", [128, 2, NCH, 128], BF16))
        bBb = E(nc.sbuf_tensor("bBb", [128, 4, 2 * QCH, 128], BF16))
        wsumq_c = E(nc.sbuf_tensor("wsumq_c", [128, QCH], F32))
        wsumb_c = E(nc.sbuf_tensor("wsumb_c", [128, NCH], F32))
        wsumar_c = E(nc.sbuf_tensor("wsumar_c", [128, 2], F32))
        negm_sb = E(nc.sbuf_tensor("negm_sb", [1, TG], BF16))
        negm_b = E(nc.sbuf_tensor("negm_b", [128, TG], BF16))
        rsb = E(nc.sbuf_tensor("rsb", [128, TG], F32))
        at_b = E(nc.sbuf_tensor("at_b", [128, NCH, 64], BF16))
        rw_b = E(nc.sbuf_tensor("rw_b", [128, NCH, 16], BF16))
        m_b = E(nc.sbuf_tensor("m_b", [128, QCH, 128], BF16))
        z_b = E(nc.sbuf_tensor("z_b", [128, QCH], BF16))
        bq_sb = E(nc.sbuf_tensor("bq_sb", [128, QCH], F32))
        bb_sb = E(nc.sbuf_tensor("bb_sb", [128, NCH], F32))
        esel_b = E(nc.sbuf_tensor("esel_b", [16, QCH, 128], BF16))
        eye_sb = E(nc.sbuf_tensor("eye_sb", [128, 128], F32))
        ones_sb = E(nc.sbuf_tensor("ones_sb", [128, 128], F32))
        ones_bb = E(nc.sbuf_tensor("ones_bb", [128, 1], BF16))
        ones_br = E(nc.sbuf_tensor("ones_br", [1, 128], BF16))
        res_rep = E(nc.sbuf_tensor("res_rep", [128, TG], BF16))
        lg_sb = E(nc.sbuf_tensor("lg_sb", [16, TG], F32))
        lgt_sb = E(nc.sbuf_tensor("lgt_sb", [128, 64], F32))
        wt_sb = E(nc.sbuf_tensor("wt_sb", [128, 64], F32))
        comb_sb = E(nc.sbuf_tensor("comb_sb", [16, TG], BF16))
        col_sb = E(nc.sbuf_tensor("col_sb", [128, 32], F32))
        msk_sb = E(nc.sbuf_tensor("msk_sb", [128, 64], F32))
        lg2_sb = E(nc.sbuf_tensor("lg2_sb", [128, 64], F32))
        e_sb = E(nc.sbuf_tensor("e_sb", [128, 64], F32))
        p_sb = E(nc.sbuf_tensor("p_sb", [128, 64], F32))
        pb = [E(nc.psum_tensor(f"pb{i}", [128, TG], F32)) for i in range(8)]

        sems = {}
        for s in SEM_INC:
            sems[s] = E(nc.semaphore(s))

        ROW = {"s1": 0, "s2": 1, "t": 2, "r": 3, "den": 0, "rden": 1}

        def row(name):
            s = ROW[name] * TG
            return rows[0:1, s:s + TG]

        rk = lambda name: ("rows", ROW[name])

        P.op("sp", "qcs", lambda e: e.dma_start(bq_sb[:, :], bq_img[:, :]), writes=[("bq",)])
        P.op("sp", "qcs", lambda e: e.dma_start(bb_sb[:, :], bb_img[:, :]), writes=[("bb",)])
        P.op("sp", "qcs", lambda e: e.dma_start(eye_sb[:, :], eye_img[:, :]), writes=[("eye",)])
        P.op("dve", "dve", lambda e: e.memset(ones_sb[:, :], 1.0), writes=[("ones",)])
        P.op("dve", "dve", lambda e: e.memset(ones_bb[:, :], 1.0), writes=[("ones_bb",)])
        P.op("dve", "dve", lambda e: e.memset(ones_br[:, :], 1.0), writes=[("ones_br",)])
        P.op("gp", "qcg", lambda e: e.dma_start(z_b[:, :], z_img[:, :]), writes=[("z_b",)])
        P.op("gp", "qcg", lambda e: e.dma_start(m_b[:, :, :], m_img[:, :, :]), writes=[("m_b",)])
        P.op("gp", "qcg", lambda e: e.dma_start(at_b[:, :, :], at_img[:, :, :]), writes=[("at_b",)])
        P.op("gp", "qcg", lambda e: e.dma_start(rw_b[:, :, :], rw_img[:, :, :]), writes=[("rw_b",)])
        P.op("gp", "qcg", lambda e: e.dma_start(esel_b[:, :, :], esel_img[:, :, :]), writes=[("esel_b",)])

        SLOT_SEM = {"wqb": "qwq", "bwb": "qbw", "bBb": "qbB"}
        SCRATCH = {"wqb": (wq_s, "sp"), "bwb": (base_s, "sp"), "bBb": (bB_s, "sp")}

        def stream_block(img, bufsb, bufkey, idx, nkchunk, p):
            rot = idx % 2
            scr, lstream = SCRATCH[bufkey]
            if p == 0:
                P.op("gp", SLOT_SEM[bufkey] + str(rot),
                     lambda e, rot=rot: e.dma_start(
                         bufsb[:, rot, 0:nkchunk, :], img[idx, :, :, :]),
                     writes=[(bufkey, rot)])
                P.op("sp", "qsc%d" % rot,
                     lambda e, rot=rot: e.dma_start(
                         scr[idx, :, :, :], bufsb[:, rot, 0:nkchunk, :]),
                     reads=[(bufkey, rot)], writes=[(bufkey + "_s", idx)])
            else:
                P.op(lstream, "h" + SLOT_SEM[bufkey][1:] + str(rot),
                     lambda e, rot=rot: e.dma_start(
                         bufsb[:, rot, 0:nkchunk, :], scr[idx, :, :, :]),
                     reads=[(bufkey + "_s", idx)], writes=[(bufkey, rot)])
            return rot

        def mm_sub(out_ap, bank, lhs_fn, rhs_fn, nk, reads_base, rhs_keys, sub=8,
                   tail=None, tail_reads=()):
            for s0 in range(0, nk, sub):
                s1 = min(s0 + sub, nk)
                P.op("pe", "pe",
                     lambda e, s0=s0, s1=s1: [
                         e.matmul(out_ap, lhs_fn(ko), rhs_fn(ko),
                                  start=(ko == 0),
                                  stop=(tail is None and ko == nk - 1))
                         for ko in range(s0, s1)][-1],
                     reads=list(reads_base) + [rhs_keys(ko) for ko in range(s0, s1)],
                     writes=[("pb", bank)])
            if tail is not None:
                P.op("pe", "pe", lambda e: tail(e),
                     reads=list(tail_reads), writes=[("pb", bank)])

        def warm(bank):
            P.op("pe", "pe",
                 lambda e, bank=bank: e.matmul(pb[bank][:, 0:128], m_b[:, 0, :], m_b[:, 1, :],
                                               start=True, stop=True),
                 reads=[("m_b",)], writes=[("pb", bank)])

        pending = []

        def flush_pending():
            while pending:
                pending.pop(0)()

        def wsum_calc(lhs_fn, nk, n_cols, reads_base, dst_ap, dst_key, bank):
            P.op("pe", "pe",
                 lambda e: [e.matmul(pb[bank][0:n_cols, 0:1], lhs_fn(ko), ones_bb[:, :],
                                     start=(ko == 0), stop=(ko == nk - 1))
                            for ko in range(nk)][-1],
                 reads=list(reads_base) + [("ones_bb",)], writes=[("pb", bank)])
            P.op("act", "act",
                 lambda e: e.copy(dst_ap, pb[bank][0:n_cols, 0:1]),
                 reads=[("pb", bank)], writes=[dst_key])

        def ln_chunk(ko, b4, b5, precast):
            b = ko % 2
            if not precast:
                P.op("act", "act",
                     lambda e, ko=ko: e.copy(xb[:, ko, :], x_sb[:, ko, :]),
                     reads=[("x", ko)], writes=[("xb", ko)])
            P.op("act", "act",
                 lambda e, ko=ko, b=b: e.square(xsq[:, b, :], x_sb[:, ko, :]),
                 reads=[("x", ko)], writes=[("xsq", b)])
            P.op("pe", "pe",
                 lambda e, ko=ko: e.matmul(pb[b4][0:1, :], ones_bb[:, :],
                                           xb[:, ko, :],
                                           start=(ko == 0), stop=(ko == NCH - 1),
                                           skip_group_check=True),
                 reads=[("xb", ko), ("ones_bb",)], writes=[("pb", b4)])
            P.op("pe", "pe",
                 lambda e, ko=ko, b=b: e.matmul(pb[b5][0:1, :], ones_bb[:, :],
                                                xsq[:, b, :],
                                                start=(ko == 0), stop=(ko == NCH - 1),
                                                skip_group_check=True),
                 reads=[("xsq", b), ("ones_bb",)], writes=[("pb", b5)])

        def ln_rows(b4, b5):
            P.op("act", "act", lambda e: e.copy(row("s1"), pb[b4][0:1, :]),
                 reads=[("pb", b4)], writes=[rk("s1")])
            P.op("act", "act", lambda e: e.copy(row("s2"), pb[b5][0:1, :]),
                 reads=[("pb", b5)], writes=[rk("s2")])
            c = 1.0 / 4096.0
            P.op("dve", "dve", lambda e: e.tensor_scalar_mul(row("s1"), row("s1"), c),
                 reads=[rk("s1")], writes=[rk("s1")])
            P.op("dve", "dve", lambda e: e.tensor_scalar_mul(row("s2"), row("s2"), c),
                 reads=[rk("s2")], writes=[rk("s2")])
            P.op("dve", "dve",
                 lambda e: e.tensor_scalar_mul(negm_sb[0:1, :], row("s1"), -1.0),
                 reads=[rk("s1")], writes=[("negm",)])
            P.op("dve", "dve", lambda e: e.tensor_mul(row("t"), row("s1"), row("s1")),
                 reads=[rk("s1")], writes=[rk("t")])
            P.op("dve", "dve", lambda e: e.tensor_sub(row("t"), row("s2"), row("t")),
                 reads=[rk("s2"), rk("t")], writes=[rk("t")])
            P.op("dve", "dve",
                 lambda e: e.tensor_scalar_add(row("t"), row("t"), 1e-5),
                 reads=[rk("t")], writes=[rk("t")])
            P.op("act", "act",
                 lambda e: e.sqrt(row("r"), row("t")),
                 reads=[rk("t")], writes=[rk("r")])
            P.op("dve", "dve", lambda e: e.reciprocal(row("r"), row("r")),
                 reads=[rk("r")], writes=[rk("r")])

        def bcast_rsb():
            P.op("pe", "pe",
                 lambda e: e.matmul(pb[3][:, :], ones_sb[0:1, :], row("r"), start=True, stop=True),
                 reads=[rk("r"), ("ones",)], writes=[("pb", 3)])
            P.op("act", "act", lambda e: e.copy(rsb[:, :], pb[3][:, :]),
                 reads=[("pb", 3)], writes=[("rsb",)])
            P.op("pe", "pe",
                 lambda e: e.matmul(pb[2][:, :], ones_br[:, :], negm_sb[0:1, :],
                                    start=True, stop=True),
                 reads=[("negm",), ("ones_br",)], writes=[("pb", 2)])
            P.op("act", "act", lambda e: e.copy(negm_b[:, :], pb[2][:, :]),
                 reads=[("pb", 2)], writes=[("negm_b",)])

        def q_phase(p):
            for mo in range(QCH):
                rot = stream_block(wq_img, wqb, "wqb", mo, NCH, p)
                bank = mo % 2
                if p == 0:
                    wsum_calc(lambda ko, rot=rot: wqb[:, rot, ko, :], NCH, 128,
                              [("wqb", rot)],
                              wsumq_c[:, mo:mo + 1], ("wsumq", mo), 5)
                mm_sub(pb[bank][:, :], bank,
                       lambda ko, rot=rot: wqb[:, rot, ko, :],
                       lambda ko: xb[:, ko, :],
                       NCH, [("wqb", rot)], lambda ko: ("xb", ko))
                if mo == 0:
                    bcast_rsb()
                P.op("dve", "dve",
                     lambda e, bank=bank, mo=mo: e.scalar_tensor_tensor(
                         tmp0[:, :], negm_b[:, :], wsumq_c[:, mo:mo + 1], pb[bank][:, :],
                         ALU.mult, ALU.add),
                     reads=[("negm_b",), ("wsumq", mo), ("pb", bank)], writes=[("tmp0",)])
                P.op("dve", "dve",
                     lambda e: e.tensor_mul(tmp0[:, :], tmp0[:, :], rsb[:, :]),
                     reads=[("rsb",), ("tmp0",)], writes=[("tmp0",)])
                P.op("act", "act",
                     lambda e, mo=mo: e.activation(
                         tmp1[:, :], tmp0[:, :], AF.Relu, bias=bq_sb[:, mo:mo + 1], scale=1.0),
                     reads=[("tmp0",), ("bq",)], writes=[("tmp1",)])
                P.op("dve", "dve",
                     lambda e, mo=mo: e.scalar_tensor_tensor(
                         tmp0[:, :], tmp0[:, :], bq_sb[:, mo:mo + 1], tmp1[:, :],
                         ALU.add, ALU.subtract),
                     reads=[("tmp0",), ("bq",), ("tmp1",)], writes=[("tmp0",)])
                P.op("act", "act",
                     lambda e: e.activation(tmp0[:, :], tmp0[:, :], AF.Exp),
                     reads=[("tmp0",)], writes=[("tmp0",)])
                P.op("dve", "dve",
                     lambda e, mo=mo: e.tensor_add(sig[:, mo, :], tmp1[:, :], tmp0[:, :]),
                     reads=[("tmp0",), ("tmp1",)], writes=[("sig", mo)])
                if mo >= 1:
                    attn_step(mo - 1)
            attn_step(QCH - 1)
            for g in (QCH - 2, QCH - 1):
                attn_bcast(g)
                attn_retr(g)

        def rowi(slot, g):
            s = (2 * (g % 2) + slot) * TG
            return rows[0:1, s:s + TG]

        def atmp(g):
            return xsq[:, g % 2, :]

        def atmpk(g):
            return ("xsq", g % 2)

        def attn_bcast(g):
            rb = 2 + g % 2
            P.op("pe", "pe",
                 lambda e, g=g, rb=rb: e.matmul(pb[rb][:, :], ones_sb[0:1, :],
                                                rowi(1, g), start=True, stop=True),
                 reads=[("rows", 2 * (g % 2) + 1), ("ones",)], writes=[("pb", rb)])

        def attn_retr(gg):
            P.op("dve", "dve",
                 lambda e, gg=gg: e.tensor_tensor(
                     retr[:, gg, :], atmp(gg),
                     pb[2 + gg % 2][:, :], ALU.mult),
                 reads=[atmpk(gg), ("pb", 2 + gg % 2)],
                 writes=[("retr", gg)])

        def attn_step(g):
            nb = 6 + g % 2
            db = 4 + g % 2
            P.op("pe", "pe",
                 lambda e, g=g, nb=nb: e.matmul(pb[nb][:, :], m_b[:, g, :], sig[:, g, :],
                                                start=True, stop=True),
                 reads=[("m_b",), ("sig", g)], writes=[("pb", nb)])
            P.op("pe", "pe",
                 lambda e, g=g, db=db: e.matmul(pb[db][0:1, :], z_b[:, g:g + 1], sig[:, g, :],
                                                start=True, stop=True),
                 reads=[("z_b",), ("sig", g)], writes=[("pb", db)])
            if g >= 2:
                attn_bcast(g - 2)
            P.op("dve", "dve",
                 lambda e, g=g, db=db: e.tensor_single_scalar(rowi(0, g), pb[db][0:1, :],
                                                              1e-6, ALU.add),
                 reads=[("pb", db)], writes=[("rows", 2 * (g % 2))])
            P.op("dve", "dve",
                 lambda e, g=g: e.reciprocal(rowi(1, g), rowi(0, g)),
                 reads=[("rows", 2 * (g % 2))], writes=[("rows", 2 * (g % 2) + 1)])
            if g >= 2:
                attn_retr(g - 2)
            P.op("act", "act",
                 lambda e, g=g, nb=nb: e.copy(atmp(g), pb[nb][:, :]),
                 reads=[("pb", nb)], writes=[atmpk(g)])

        def stream_block_wo2(ob):
            rot = ob % 2
            P.op("gp", "qbw" + str(rot),
                 lambda e, rot=rot, ob=ob: e.dma_start(
                     bwb[:, rot, :, :], wo2_img[ob, :, :, :]),
                 writes=[("bwb", rot)])
            return rot

        def ao_phase(p):
            woe_r = woe_s.ap()
            for ob in range(NCH):
                bank = (0, 1, 4)[ob % 3]
                if p == 0:
                    rot = stream_block_wo2(ob)
                    mm_sub(pb[bank][:, :], bank,
                           lambda ko, rot=rot: bwb[:, rot, ko, :],
                           lambda ko: retr[:, ko // 4, :],
                           NCH, [("bwb", rot)],
                           lambda ko: ("retr", ko // 4))
                else:
                    prot = (ob // 4) % 2
                    if ob % 4 == 0:
                        P.op("sp", "qwo%d" % prot,
                             lambda e, ob=ob, prot=prot: e.dma_start(
                                 bwb[:, prot, :, :].rearrange("p (o g) m -> p o g m", o=4),
                                 woe_s[ob:ob + 4, :, :, :].rearrange("o k g m -> k o g m")),
                             reads=[("woe_s", ob + i, g) for i in range(4) for g in range(QCH)],
                             writes=[("bwb", prot)])
                    half = QCH * (ob % 4)
                    P.op("pe", "pe",
                         lambda e, prot=prot, bank=bank, half=half: [
                             e.matmul(pb[bank][:, :], bwb[:, prot, half + g, :], retr[:, g, :],
                                      start=(g == 0), stop=(g == QCH - 1))
                             for g in range(QCH)][-1],
                         reads=[("bwb", prot)] + [("retr", g) for g in range(QCH)],
                         writes=[("pb", bank)])
                P.op("dve", "dve",
                     lambda e, ob=ob, bank=bank: e.tensor_tensor(
                         x_sb[:, ob, :], x_sb[:, ob, :], pb[bank][:, :], ALU.add),
                     reads=[("x", ob), ("pb", bank)], writes=[("x", ob)])
                if p > 0:
                    warm(5)
                if ob >= 2:
                    ln_chunk(ob - 2, 6, 7, precast=False)
                if p == 0:
                    rot = ob % 2
                    wv = bBb[:, :, 0:QCH, :]
                    P.op("dve", "dve",
                         lambda e, rot=rot, wv=wv: e.tensor_add(
                             wv[:, 0, :, :], bwb[:, rot, 0:32:4, :], bwb[:, rot, 1:32:4, :]),
                         reads=[("bwb", rot)], writes=[("bBb", 0)])
                    P.op("dve", "dve",
                         lambda e, rot=rot, wv=wv: e.tensor_add(
                             wv[:, 1, :, :], bwb[:, rot, 2:32:4, :], bwb[:, rot, 3:32:4, :]),
                         reads=[("bwb", rot)], writes=[("bBb", 1)])
                    P.op("dve", "dve",
                         lambda e, wv=wv: e.tensor_tensor(
                             wv[:, 0, :, :], wv[:, 0, :, :], wv[:, 1, :, :], ALU.add),
                         reads=[("bBb", 0), ("bBb", 1)], writes=[("bBb", 0)])
                    P.op("sp", "qps%d" % (ob % 2),
                         lambda e, ob=ob, wv=wv: e.dma_start(
                             woe_r[ob, :, :, :], wv[:, 0, :, :]),
                         reads=[("bBb", 0)],
                         writes=[("woe_s", ob, g) for g in range(QCH)])
            ln_chunk(NCH - 2, 6, 7, precast=False)
            ln_chunk(NCH - 1, 6, 7, precast=False)
            ln_rows(6, 7)

        def ffn_matmuls(p):
            if p == 0:
                wsum_calc(lambda ko: at_b[:, ko, :], NCH, 64, [("at_b",)],
                          wsumar_c[0:64, 0:1], ("wsumA",), 6)
                wsum_calc(lambda ko: rw_b[:, ko, :], NCH, 16, [("rw_b",)],
                          wsumar_c[0:16, 1:2], ("wsumR",), 7)
            mm_sub(pb[5][0:64, :], 5,
                   lambda ko: at_b[:, ko, :], lambda ko: xb[:, ko, :],
                   NCH, [("at_b",)], lambda ko: ("xb", ko))
            bcast_rsb()
            P.op("dve", "dve",
                 lambda e: e.scalar_tensor_tensor(
                     tmp0[0:64, :], negm_b[0:64, :], wsumar_c[0:64, 0:1], pb[5][0:64, :],
                     ALU.mult, ALU.add),
                 reads=[("negm_b",), ("wsumA",), ("pb", 5)], writes=[("tmp0",)])
            P.op("dve", "dve",
                 lambda e: e.tensor_tensor(res_rep[0:64, :], rsb[0:64, :], tmp0[0:64, :], ALU.mult),
                 reads=[("rsb",), ("tmp0",)], writes=[("res_rep",)])
            P.op("act", "act", lambda e: e.copy(res_rep[64:128, :], res_rep[0:64, :]),
                 reads=[("res_rep",)], writes=[("res_rep",)])
            mm_sub(pb[4][0:16, :], 4,
                   lambda ko: rw_b[:, ko, :], lambda ko: xb[:, ko, :],
                   NCH, [("rw_b",)], lambda ko: ("xb", ko))
            P.op("dve", "dve",
                 lambda e: e.scalar_tensor_tensor(
                     tmp0[0:16, :], negm_b[0:16, :], wsumar_c[0:16, 1:2], pb[4][0:16, :],
                     ALU.mult, ALU.add),
                 reads=[("negm_b",), ("wsumR",), ("pb", 4)], writes=[("tmp0",)])
            P.op("dve", "dve",
                 lambda e: e.tensor_tensor(lg_sb[:, :], rsb[0:16, :], tmp0[0:16, :], ALU.mult),
                 reads=[("rsb",), ("tmp0",)], writes=[("lg",)])

        def router_fwd_transposes(p):
            for j in range(4):
                bank = 6 + j % 2
                P.op("pe", "pe",
                     lambda e, j=j, bank=bank: e.transpose(
                         pb[bank][:, 0:16], lg_sb[:, 128 * j:128 * (j + 1)], eye_sb[0:16, 0:16]),
                     reads=[("lg",), ("eye",)], writes=[("pb", bank)])
                P.op("act", "act",
                     lambda e, j=j, bank=bank: e.copy(lgt_sb[:, 16 * j:16 * (j + 1)], pb[bank][:, 0:16]),
                     reads=[("pb", bank)], writes=[("lgt", j)])

        def router_topk(p):
            def cl(j, i):
                return col_sb[:, 8 * j + i:8 * j + i + 1]

            def sl(buf, j):
                return buf[:, 16 * j:16 * (j + 1)]

            J = range(4)
            for j in J:
                P.op("dve", "dve", lambda e, j=j: e.reduce_max(cl(j, 0), sl(lgt_sb, j), axis=AX.X),
                     reads=[("lgt", j)], writes=[("col", j, 0)])
            for j in J:
                P.op("dve", "dve", lambda e, j=j: e.tensor_scalar_mul(cl(j, 1), cl(j, 0), -1.0),
                     reads=[("col", j, 0)], writes=[("col", j, 1)])
                P.op("dve", "dve",
                     lambda e, j=j: e.tensor_single_scalar(sl(msk_sb, j), sl(lgt_sb, j), cl(j, 0), ALU.is_ge),
                     reads=[("lgt", j), ("col", j, 0)], writes=[("msk", j)])
            for j in J:
                P.op("dve", "dve",
                     lambda e, j=j: e.scalar_tensor_tensor(
                         sl(lg2_sb, j), sl(msk_sb, j), -1e30, sl(lgt_sb, j), ALU.mult, ALU.add),
                     reads=[("msk", j), ("lgt", j)], writes=[("lg2", j)])
                P.op("act", "act",
                     lambda e, j=j: e.activation(sl(e_sb, j), sl(lgt_sb, j), AF.Exp, bias=cl(j, 1), scale=1.0),
                     reads=[("lgt", j), ("col", j, 1)], writes=[("e_sb", j)])
            for j in J:
                P.op("dve", "dve", lambda e, j=j: e.reduce_max(cl(j, 2), sl(lg2_sb, j), axis=AX.X),
                     reads=[("lg2", j)], writes=[("col", j, 2)])
                P.op("dve", "dve", lambda e, j=j: e.reduce_sum(cl(j, 3), sl(e_sb, j), axis=AX.X),
                     reads=[("e_sb", j)], writes=[("col", j, 3)])
            for j in J:
                P.op("dve", "dve",
                     lambda e, j=j: e.tensor_single_scalar(sl(lg2_sb, j), sl(lg2_sb, j), cl(j, 2), ALU.is_ge),
                     reads=[("lg2", j), ("col", j, 2)], writes=[("lg2", j)])
                P.op("dve", "dve", lambda e, j=j: e.reciprocal(cl(j, 4), cl(j, 3)),
                     reads=[("col", j, 3)], writes=[("col", j, 4)])
            for j in J:
                P.op("dve", "dve",
                     lambda e, j=j: e.tensor_add(sl(msk_sb, j), sl(msk_sb, j), sl(lg2_sb, j)),
                     reads=[("msk", j), ("lg2", j)], writes=[("msk", j)])
                P.op("dve", "dve",
                     lambda e, j=j: e.tensor_single_scalar(sl(p_sb, j), sl(e_sb, j), cl(j, 4), ALU.mult),
                     reads=[("e_sb", j), ("col", j, 4)], writes=[("p_sb", j)])
            for j in J:
                P.op("dve", "dve",
                     lambda e, j=j: e.tensor_mul(sl(p_sb, j), sl(p_sb, j), sl(msk_sb, j)),
                     reads=[("p_sb", j), ("msk", j)], writes=[("p_sb", j)])
            for j in J:
                P.op("dve", "dve", lambda e, j=j: e.reduce_sum(cl(j, 5), sl(p_sb, j), axis=AX.X),
                     reads=[("p_sb", j)], writes=[("col", j, 5)])
            for j in J:
                P.op("dve", "dve",
                     lambda e, j=j: e.tensor_scalar(cl(j, 6), cl(j, 5), 2.0, 2e-6, ALU.mult, ALU.add),
                     reads=[("col", j, 5)], writes=[("col", j, 6)])
            for j in J:
                P.op("dve", "dve", lambda e, j=j: e.reciprocal(cl(j, 6), cl(j, 6)),
                     reads=[("col", j, 6)], writes=[("col", j, 6)])
            for j in J:
                P.op("dve", "dve",
                     lambda e, j=j: e.tensor_single_scalar(
                         sl(wt_sb, j), sl(p_sb, j), cl(j, 6), ALU.mult),
                     reads=[("p_sb", j), ("col", j, 6)], writes=[("wt", j)])

        def router_back(p):
            for j in range(4):
                bank = 6 + j % 2
                P.op("pe", "pe",
                     lambda e, j=j, bank=bank: e.transpose(
                         pb[bank][0:16, 0:128], wt_sb[:, 16 * j:16 * (j + 1)], eye_sb[:, :]),
                     reads=[("wt", j), ("eye",)], writes=[("pb", bank)])
                P.op("act", "act",
                     lambda e, j=j, bank=bank: e.copy(comb_sb[:, 128 * j:128 * (j + 1)], pb[bank][0:16, 0:128]),
                     reads=[("pb", bank)], writes=[("comb",)])

        def wh_phase(p):
            for g8 in range(QCH):
                bank = 6 + g8 % 2
                P.op("pe", "pe",
                     lambda e, g8=g8, bank=bank: e.matmul(
                         pb[bank][:, :], esel_b[:, g8, :], comb_sb[:, :], start=True, stop=True),
                     reads=[("esel_b",), ("comb",)], writes=[("pb", bank)])
                P.op("dve", "dve",
                     lambda e, g8=g8, bank=bank: e.tensor_tensor(
                         retr[:, g8, :], res_rep[:, :], pb[bank][:, :], ALU.mult),
                     reads=[("res_rep",), ("pb", bank)], writes=[("retr", g8)])

        def out_base(p, ob):
            rotw = stream_block(base_img, bwb, "bwb", ob, NCH, p)
            bank = (0, 1, 4)[ob % 3]
            if p == 0:
                wsum_calc(lambda ko, rotw=rotw: bwb[:, rotw, ko, :], NCH, 128,
                          [("bwb", rotw)],
                          wsumb_c[:, ob:ob + 1], ("wsumb", ob), 2 + ob % 2)
            mm_sub(pb[bank][:, :], bank,
                   lambda ko, rotw=rotw: bwb[:, rotw, ko, :],
                   lambda ko: xb[:, ko, :],
                   NCH, [("bwb", rotw)], lambda ko: ("xb", ko))
            tb, tk = (tmp0, ("tmp0",)) if ob % 2 == 0 else (tmp1, ("tmp1",))
            P.op("dve", "dve",
                 lambda e, ob=ob, bank=bank, tb=tb: e.scalar_tensor_tensor(
                     tb[:, :], negm_b[:, :], wsumb_c[:, ob:ob + 1], pb[bank][:, :],
                     ALU.mult, ALU.add),
                 reads=[("negm_b",), ("wsumb", ob), ("pb", bank)], writes=[tk])
            P.op("dve", "dve",
                 lambda e, tb=tb: e.tensor_mul(tb[:, :], tb[:, :], rsb[:, :]),
                 reads=[("rsb",), tk], writes=[tk])
            P.op("dve", "dve",
                 lambda e, ob=ob, tb=tb: e.scalar_tensor_tensor(
                     x_sb[:, ob, :], tb[:, :], bb_sb[:, ob:ob + 1], x_sb[:, ob, :],
                     ALU.add, ALU.add),
                 reads=[tk, ("bb",), ("x", ob)], writes=[("x", ob)])

        def out_delta(p, ob):
            prot = (ob // 2) % 4
            if ob % 2 == 0:
                if p == 0:
                    P.op("gp", "qbB%d" % prot,
                         lambda e, ob=ob, prot=prot: e.dma_start(
                             bBb[:, prot, :, :].rearrange("p (o g) m -> p o g m", o=2),
                             bB_img[ob:ob + 2, :, :, :].rearrange("o k g m -> k o g m")),
                         writes=[("bBb", prot)])
                    P.op("sp", "qsc%d" % (prot % 2),
                         lambda e, ob=ob, prot=prot: e.dma_start(
                             bB_s[ob:ob + 2, :, :, :].rearrange("o k g m -> k o g m"),
                             bBb[:, prot, :, :].rearrange("p (o g) m -> p o g m", o=2)),
                         reads=[("bBb", prot)],
                         writes=[("bBb_s", ob)])
                else:
                    P.op("sp", "hbB%d" % prot,
                         lambda e, ob=ob, prot=prot: e.dma_start(
                             bBb[:, prot, :, :].rearrange("p (o g) m -> p o g m", o=2),
                             bB_s[ob:ob + 2, :, :, :].rearrange("o k g m -> k o g m")),
                         reads=[("bBb_s", ob)], writes=[("bBb", prot)])
            half = QCH * (ob % 2)
            dbank = 2 + ob % 2
            P.op("pe", "pe",
                 lambda e, prot=prot, dbank=dbank, half=half: [
                     e.matmul(pb[dbank][:, :], bBb[:, prot, half + g, :], retr[:, g, :],
                              start=(g == 0), stop=(g == QCH - 1))
                     for g in range(QCH)][-1],
                 reads=[("bBb", prot)] + [("retr", g) for g in range(QCH)],
                 writes=[("pb", dbank)])
            ro = ob % 3
            P.op("dve", "dve",
                 lambda e, ob=ob, ro=ro, dbank=dbank: e.tensor_tensor(
                     outb[:, ro, :], x_sb[:, ob, :], pb[dbank][:, :], ALU.add),
                 reads=[("x", ob), ("pb", dbank), ("outb", ro)], writes=[("outb", ro)])
            P.op("act", "qo%d" % ro,
                 lambda e, p=p, ob=ob, ro=ro: e.dma_start(out_img[p, ob, :, :], outb[:, ro, :]),
                 reads=[("outb", ro)])

        for p in range(npass):
            if p == 0:
                for c in range(8):
                    P.op("act", "qx%d" % (c % 4),
                         lambda e, c=c: e.dma_start(
                             x_sb[:, 4 * c:4 * (c + 1), :], x_img[0, :, 4 * c:4 * (c + 1), :]),
                         writes=[("x", ko) for ko in range(4 * c, 4 * (c + 1))])
            if p == 0:
                for ko in range(NCH):
                    ln_chunk(ko, 4, 5, precast=False)
                ln_rows(4, 5)
            q_phase(p)
            ao_phase(p)
            ffn_matmuls(p)
            out_base(p, 0)
            out_base(p, 1)
            router_fwd_transposes(p)
            for ob in range(2, 10):
                out_base(p, ob)
            router_topk(p)
            for ob in range(10, 14):
                out_base(p, ob)
            router_back(p)
            wh_phase(p)
            for ob in range(14, NCH):
                out_base(p, ob)
            for ob in range(NCH):
                out_delta(p, ob)
                warm(0)
                if p + 1 < npass and ob % 4 == 3:
                    c = ob // 4
                    P.op("act", "qx%d" % (c % 4),
                         lambda e, p=p, c=c: e.dma_start(
                             x_sb[:, 4 * c:4 * (c + 1), :], x_img[p + 1, :, 4 * c:4 * (c + 1), :]),
                         writes=[("x", ko) for ko in range(4 * c, 4 * (c + 1))])
                    if c >= 2:
                        for ko in range(4 * (c - 2), 4 * (c - 1)):
                            P.op("act", "act",
                                 lambda e, ko=ko: e.copy(xb[:, ko, :], x_sb[:, ko, :]),
                                 reads=[("x", ko)], writes=[("xb", ko)])
                            ln_chunk(ko, 4, 5, precast=True)
            if p + 1 < npass:
                for ko in range(24, 32):
                    P.op("act", "act",
                         lambda e, ko=ko: e.copy(xb[:, ko, :], x_sb[:, ko, :]),
                         reads=[("x", ko)], writes=[("xb", ko)])
                    ln_chunk(ko, 4, 5, precast=True)
                    warm(0)
                ln_rows(4, 5)

        P.wait("sp", "qo0", P.cnt["qo0"])
        P.wait("sp", "qo1", P.cnt["qo1"])

        with nc.Block() as block:
            @block.tensor
            def _(t):
                P.run_stream("pe", t, sems)

            @block.scalar
            def _(s):
                P.run_stream("act", s, sems)

            @block.vector
            def _(v):
                P.run_stream("dve", v, sems)

            @block.gpsimd
            def _(g):
                P.run_stream("gp", g, sems)

            @block.sync
            def _(s):
                P.run_stream("sp", s, sems)

    return nc


def stage_inputs(inputs, npass, tokens_per_core, ncores):
    x = np.ascontiguousarray(np.asarray(inputs["L0_attn_hidden_states"], np.float32).reshape(-1, 4096))
    Wq = np.asarray(inputs["L0_attn_Wq"], np.float32)
    bq = np.asarray(inputs["L0_attn_bq"], np.float32)
    Wo = np.asarray(inputs["L0_attn_Wo"], np.float32)
    M = np.asarray(inputs["L0_attn_M"], np.float32)
    z = np.asarray(inputs["L0_attn_z"], np.float32)
    base_w = np.asarray(inputs["L0_ffn_base_w"], np.float32)
    base_b = np.asarray(inputs["L0_ffn_base_b"], np.float32)
    A = np.asarray(inputs["L0_ffn_A"], np.float32)
    B = np.asarray(inputs["L0_ffn_B"], np.float32)
    router_w = np.asarray(inputs["L0_ffn_router_w"], np.float32)

    sel = np.concatenate([np.arange(512 * g, 512 * g + 128) for g in range(8)])
    WqselT = np.ascontiguousarray(Wq.T[:, sel])
    wq_img = np.ascontiguousarray(
        WqselT.reshape(NCH, 128, QCH, 128).transpose(2, 1, 0, 3))
    base_img = np.ascontiguousarray(
        np.ascontiguousarray(base_w.T).reshape(NCH, 128, NCH, 128).transpose(2, 1, 0, 3))
    B_flatT = np.ascontiguousarray(np.transpose(B, (0, 2, 1)).reshape(1024, 4096))
    bB_img = np.ascontiguousarray(
        B_flatT.reshape(QCH, 128, NCH, 128).transpose(2, 1, 0, 3))
    wo2_img = np.ascontiguousarray(
        np.ascontiguousarray(Wo.T).reshape(NCH, 128, NCH, 128).transpose(2, 1, 0, 3))
    at_img = np.ascontiguousarray(A.T.reshape(NCH, 128, 64).transpose(1, 0, 2))
    rw_img = np.ascontiguousarray(router_w.T.reshape(NCH, 128, 16).transpose(1, 0, 2))
    m_img = np.ascontiguousarray(M.transpose(1, 0, 2))
    z_img = np.ascontiguousarray(z.T)
    bq_img = np.ascontiguousarray(bq[sel].reshape(QCH, 128).T)
    bb_img = np.ascontiguousarray(base_b.reshape(NCH, 128).T)
    esel = np.zeros((16, QCH, 128), np.float32)
    for g8 in range(QCH):
        for m in range(128):
            esel[2 * g8 + m // 64, g8, m] = 1.0
    eye_img = np.eye(128, dtype=np.float32)

    shared = dict(wq_img=wq_img, base_img=base_img, bB_img=bB_img, wo2_img=wo2_img,
                  at_img=at_img, rw_img=rw_img, m_img=m_img, z_img=z_img,
                  bq_img=bq_img, bb_img=bb_img, esel_img=esel, eye_img=eye_img)
    in_maps = []
    for c in range(ncores):
        xc = x[c * tokens_per_core:(c + 1) * tokens_per_core]
        x_img = np.ascontiguousarray(
            xc.T.reshape(NCH, 128, npass, TG).transpose(2, 1, 0, 3))
        in_maps.append(dict(shared, x_img=x_img))
    return in_maps


def unstage_output(results, npass, tokens_per_core, ncores, out_shape):
    outs = []
    for c in range(ncores):
        oi = np.asarray(results[c]["out_img"], np.float32)
        xc = oi.transpose(1, 2, 0, 3).reshape(4096, tokens_per_core).T
        outs.append(xc)
    return np.concatenate(outs, axis=0).reshape(out_shape)


def kernel(L0_attn_hidden_states, L0_attn_Wq, L0_attn_bq, L0_attn_Wk, L0_attn_bk,
           L0_attn_Wv, L0_attn_bv, L0_attn_Wo, L0_attn_M, L0_attn_z,
           L0_ffn_base_w, L0_ffn_base_b, L0_ffn_A, L0_ffn_B, L0_ffn_router_w):
    _ = (L0_attn_Wk, L0_attn_bk, L0_attn_Wv, L0_attn_bv)
    inputs = dict(
        L0_attn_hidden_states=L0_attn_hidden_states, L0_attn_Wq=L0_attn_Wq,
        L0_attn_bq=L0_attn_bq, L0_attn_Wo=L0_attn_Wo, L0_attn_M=L0_attn_M,
        L0_attn_z=L0_attn_z, L0_ffn_base_w=L0_ffn_base_w,
        L0_ffn_base_b=L0_ffn_base_b, L0_ffn_A=L0_ffn_A, L0_ffn_B=L0_ffn_B,
        L0_ffn_router_w=L0_ffn_router_w)
    npass = NPASS_FULL
    tokens_per_core = npass * TG
    nc = build_nc(npass)
    in_maps = stage_inputs(inputs, npass, tokens_per_core, NCORES)
    res = run_bass_kernel_spmd(nc, in_maps, core_ids=list(range(NCORES)))
    out_shape = np.asarray(L0_attn_hidden_states).shape
    return unstage_output(res.results, npass, tokens_per_core, NCORES, out_shape)


# version 23 (final)
# speedup vs baseline: 1.0103x; 1.0103x over previous
from contextlib import ExitStack

import numpy as np

import concourse.bass as bass
import concourse.mybir as mybir
from concourse.bass_utils import run_bass_kernel_spmd

F32 = mybir.dt.float32
BF16 = mybir.dt.bfloat16
AF = mybir.ActivationFunctionType
ALU = mybir.AluOpType
AX = mybir.AxisListType

NCORES = 8
TG = 512
NCH = 32
QCH = 8
NPASS_FULL = 4

SEM_INC = {"pe": 1, "act": 1, "dve": 1,
           "qcs": 16, "qcg": 16, "qx0": 16, "qx1": 16, "qx2": 16, "qx3": 16, "qo0": 16, "qo1": 16, "qo2": 16,
           "qwq0": 16, "qwq1": 16, "qbw0": 16, "qbw1": 16, "qbB0": 16, "qbB1": 16, "qbB2": 16, "qbB3": 16,
           "qwo0": 16, "qwo1": 16, "qwo2": 16, "qps0": 16, "qps1": 16, "qsc0": 16, "qsc1": 16,
           "hwq0": 16, "hwq1": 16, "hbw0": 16, "hbw1": 16, "hbB0": 16, "hbB1": 16, "hbB2": 16, "hbB3": 16}
STREAMS = ("pe", "act", "dve", "gp", "sp")


class Op:
    __slots__ = ("waits", "emit", "sem", "inc")

    def __init__(self, waits, emit, sem, inc):
        self.waits, self.emit, self.sem, self.inc = waits, emit, sem, inc


class Prog:

    def __init__(self):
        self.ops = {s: [] for s in STREAMS}
        self.cnt = {s: 0 for s in SEM_INC}
        self.knows = {s: {} for s in STREAMS}
        self.res = {}

    def op(self, stream, sem, emit, reads=(), writes=()):
        needs = {}

        def need(e, c):
            if c > needs.get(e, 0):
                needs[e] = c

        for k in reads:
            st = self.res.get(k)
            if st and st[0]:
                need(*st[0])
        for k in writes:
            st = self.res.get(k)
            if st:
                if st[0]:
                    need(*st[0])
                for e, c in st[1].items():
                    need(e, c)
        if SEM_INC[sem] == 16 and self.cnt[sem] > 0:
            need(sem, self.cnt[sem])
        waits = []
        for e, c in sorted(needs.items()):
            if e == sem and e == "pe":
                continue
            if self.knows[stream].get(e, 0) >= c:
                continue
            waits.append((e, c))
            self.knows[stream][e] = c
        self.cnt[sem] += SEM_INC[sem]
        after = (sem, self.cnt[sem])
        for k in reads:
            st = self.res.setdefault(k, [None, {}])
            st[1][sem] = after[1]
        for k in writes:
            self.res[k] = [after, {}]
        self.ops[stream].append(Op(waits, emit, sem, SEM_INC[sem]))
        return after

    def wait(self, stream, sem, value):
        self.ops[stream].append(Op([(sem, value)], None, None, 0))

    def run_stream(self, stream, eng, sems):
        for o in self.ops[stream]:
            for s, v in o.waits:
                eng.wait_ge(sems[s], v)
            if o.emit is not None:
                inst = o.emit(eng)
                inst.then_inc(sems[o.sem], o.inc)


def build_nc(npass):
    nc = bass.Bass("TRN2", target_bir_lowering=False)

    x_img = nc.dram_tensor("x_img", [npass, 128, NCH, TG], F32, kind="ExternalInput")
    wq_img = nc.dram_tensor("wq_img", [QCH, 128, NCH, 128], F32, kind="ExternalInput")
    base_img = nc.dram_tensor("base_img", [NCH, 128, NCH, 128], F32, kind="ExternalInput")
    bB_img = nc.dram_tensor("bB_img", [NCH, 128, QCH, 128], F32, kind="ExternalInput")
    wo2_img = nc.dram_tensor("wo2_img", [NCH, 128, NCH, 128], F32, kind="ExternalInput")
    at_img = nc.dram_tensor("at_img", [128, NCH, 64], F32, kind="ExternalInput")
    rw_img = nc.dram_tensor("rw_img", [128, NCH, 16], F32, kind="ExternalInput")
    m_img = nc.dram_tensor("m_img", [128, QCH, 128], F32, kind="ExternalInput")
    z_img = nc.dram_tensor("z_img", [128, QCH], F32, kind="ExternalInput")
    bq_img = nc.dram_tensor("bq_img", [128, QCH], F32, kind="ExternalInput")
    bb_img = nc.dram_tensor("bb_img", [128, NCH], F32, kind="ExternalInput")
    esel_img = nc.dram_tensor("esel_img", [16, QCH, 128], F32, kind="ExternalInput")
    eye_img = nc.dram_tensor("eye_img", [128, 128], F32, kind="ExternalInput")
    out_img = nc.dram_tensor("out_img", [npass, NCH, 128, TG], F32, kind="ExternalOutput")

    woe_s = nc.dram_tensor("woe_s", [NCH, 128, QCH, 128], BF16, kind="Internal")
    wq_s = nc.dram_tensor("wq_s", [QCH, 128, NCH, 128], BF16, kind="Internal")
    base_s = nc.dram_tensor("base_s", [NCH, 128, NCH, 128], BF16, kind="Internal")
    bB_s = nc.dram_tensor("bB_s", [NCH, 128, QCH, 128], BF16, kind="Internal")

    P = Prog()

    with ExitStack() as ctx:
        E = ctx.enter_context
        x_sb = E(nc.sbuf_tensor("x_sb", [128, NCH, TG], F32))
        xb = E(nc.sbuf_tensor("xb", [128, NCH, TG], BF16))
        sig = E(nc.sbuf_tensor("sig", [128, QCH, TG], BF16))
        retr = E(nc.sbuf_tensor("retr", [128, QCH, TG], BF16))
        xsq = E(nc.sbuf_tensor("xsq", [128, 2, TG], BF16))
        tmp0 = E(nc.sbuf_tensor("tmp0", [128, TG], F32))
        tmp1 = E(nc.sbuf_tensor("tmp1", [128, TG], F32))
        outb = E(nc.sbuf_tensor("outb", [128, 3, TG], F32))
        rows = E(nc.sbuf_tensor("rows", [1, 4 * TG], F32))
        wqb = E(nc.sbuf_tensor("wqb", [128, 2, NCH, 128], BF16))
        bwb = E(nc.sbuf_tensor("bwb", [128, 2, NCH, 128], BF16))
        bBb = E(nc.sbuf_tensor("bBb", [128, 4, 2 * QCH, 128], BF16))
        wsumq_c = E(nc.sbuf_tensor("wsumq_c", [128, QCH], F32))
        wsumb_c = E(nc.sbuf_tensor("wsumb_c", [128, NCH], F32))
        wsumar_c = E(nc.sbuf_tensor("wsumar_c", [128, 2], F32))
        negm_sb = E(nc.sbuf_tensor("negm_sb", [1, TG], BF16))
        negm_b = E(nc.sbuf_tensor("negm_b", [128, TG], BF16))
        rsb = E(nc.sbuf_tensor("rsb", [128, TG], F32))
        at_b = E(nc.sbuf_tensor("at_b", [128, NCH, 64], BF16))
        rw_b = E(nc.sbuf_tensor("rw_b", [128, NCH, 16], BF16))
        m_b = E(nc.sbuf_tensor("m_b", [128, QCH, 128], BF16))
        z_b = E(nc.sbuf_tensor("z_b", [128, QCH], BF16))
        bq_sb = E(nc.sbuf_tensor("bq_sb", [128, QCH], F32))
        bb_sb = E(nc.sbuf_tensor("bb_sb", [128, NCH], F32))
        esel_b = E(nc.sbuf_tensor("esel_b", [16, QCH, 128], BF16))
        eye_sb = E(nc.sbuf_tensor("eye_sb", [128, 128], F32))
        ones_sb = E(nc.sbuf_tensor("ones_sb", [128, 128], F32))
        ones_bb = E(nc.sbuf_tensor("ones_bb", [128, 1], BF16))
        ones_br = E(nc.sbuf_tensor("ones_br", [1, 128], BF16))
        res_rep = E(nc.sbuf_tensor("res_rep", [128, TG], BF16))
        lg_sb = E(nc.sbuf_tensor("lg_sb", [16, TG], F32))
        lgt_sb = E(nc.sbuf_tensor("lgt_sb", [128, 64], F32))
        wt_sb = E(nc.sbuf_tensor("wt_sb", [128, 64], F32))
        comb_sb = E(nc.sbuf_tensor("comb_sb", [16, TG], BF16))
        col_sb = E(nc.sbuf_tensor("col_sb", [128, 32], F32))
        msk_sb = E(nc.sbuf_tensor("msk_sb", [128, 64], F32))
        lg2_sb = E(nc.sbuf_tensor("lg2_sb", [128, 64], F32))
        e_sb = E(nc.sbuf_tensor("e_sb", [128, 64], F32))
        p_sb = E(nc.sbuf_tensor("p_sb", [128, 64], F32))
        pb = [E(nc.psum_tensor(f"pb{i}", [128, TG], F32)) for i in range(8)]

        sems = {}
        for s in SEM_INC:
            sems[s] = E(nc.semaphore(s))

        ROW = {"s1": 0, "s2": 1, "t": 2, "r": 3, "den": 0, "rden": 1}

        def row(name):
            s = ROW[name] * TG
            return rows[0:1, s:s + TG]

        rk = lambda name: ("rows", ROW[name])

        P.op("sp", "qcs", lambda e: e.dma_start(bq_sb[:, :], bq_img[:, :]), writes=[("bq",)])
        P.op("sp", "qcs", lambda e: e.dma_start(bb_sb[:, :], bb_img[:, :]), writes=[("bb",)])
        P.op("sp", "qcs", lambda e: e.dma_start(eye_sb[:, :], eye_img[:, :]), writes=[("eye",)])
        P.op("dve", "dve", lambda e: e.memset(ones_sb[:, :], 1.0), writes=[("ones",)])
        P.op("dve", "dve", lambda e: e.memset(ones_bb[:, :], 1.0), writes=[("ones_bb",)])
        P.op("dve", "dve", lambda e: e.memset(ones_br[:, :], 1.0), writes=[("ones_br",)])
        P.op("gp", "qcg", lambda e: e.dma_start(z_b[:, :], z_img[:, :]), writes=[("z_b",)])
        P.op("gp", "qcg", lambda e: e.dma_start(m_b[:, :, :], m_img[:, :, :]), writes=[("m_b",)])
        P.op("gp", "qcg", lambda e: e.dma_start(at_b[:, :, :], at_img[:, :, :]), writes=[("at_b",)])
        P.op("gp", "qcg", lambda e: e.dma_start(rw_b[:, :, :], rw_img[:, :, :]), writes=[("rw_b",)])
        P.op("gp", "qcg", lambda e: e.dma_start(esel_b[:, :, :], esel_img[:, :, :]), writes=[("esel_b",)])

        SLOT_SEM = {"wqb": "qwq", "bwb": "qbw", "bBb": "qbB"}
        SCRATCH = {"wqb": (wq_s, "sp"), "bwb": (base_s, "sp"), "bBb": (bB_s, "sp")}

        def stream_block(img, bufsb, bufkey, idx, nkchunk, p):
            rot = idx % 2
            scr, lstream = SCRATCH[bufkey]
            if p == 0:
                P.op("gp", SLOT_SEM[bufkey] + str(rot),
                     lambda e, rot=rot: e.dma_start(
                         bufsb[:, rot, 0:nkchunk, :], img[idx, :, :, :]),
                     writes=[(bufkey, rot)])
                P.op("sp", "qsc%d" % rot,
                     lambda e, rot=rot: e.dma_start(
                         scr[idx, :, :, :], bufsb[:, rot, 0:nkchunk, :]),
                     reads=[(bufkey, rot)], writes=[(bufkey + "_s", idx)])
            else:
                P.op(lstream, "h" + SLOT_SEM[bufkey][1:] + str(rot),
                     lambda e, rot=rot: e.dma_start(
                         bufsb[:, rot, 0:nkchunk, :], scr[idx, :, :, :]),
                     reads=[(bufkey + "_s", idx)], writes=[(bufkey, rot)])
            return rot

        def mm_sub(out_ap, bank, lhs_fn, rhs_fn, nk, reads_base, rhs_keys, sub=8,
                   tail=None, tail_reads=()):
            for s0 in range(0, nk, sub):
                s1 = min(s0 + sub, nk)
                P.op("pe", "pe",
                     lambda e, s0=s0, s1=s1: [
                         e.matmul(out_ap, lhs_fn(ko), rhs_fn(ko),
                                  start=(ko == 0),
                                  stop=(tail is None and ko == nk - 1))
                         for ko in range(s0, s1)][-1],
                     reads=list(reads_base) + [rhs_keys(ko) for ko in range(s0, s1)],
                     writes=[("pb", bank)])
            if tail is not None:
                P.op("pe", "pe", lambda e: tail(e),
                     reads=list(tail_reads), writes=[("pb", bank)])

        def warm(bank):
            P.op("pe", "pe",
                 lambda e, bank=bank: e.matmul(pb[bank][:, 0:128], m_b[:, 0, :], m_b[:, 1, :],
                                               start=True, stop=True),
                 reads=[("m_b",)], writes=[("pb", bank)])

        pending = []

        def flush_pending():
            while pending:
                pending.pop(0)()

        def wsum_calc(lhs_fn, nk, n_cols, reads_base, dst_ap, dst_key, bank):
            P.op("pe", "pe",
                 lambda e: [e.matmul(pb[bank][0:n_cols, 0:1], lhs_fn(ko), ones_bb[:, :],
                                     start=(ko == 0), stop=(ko == nk - 1))
                            for ko in range(nk)][-1],
                 reads=list(reads_base) + [("ones_bb",)], writes=[("pb", bank)])
            P.op("act", "act",
                 lambda e: e.copy(dst_ap, pb[bank][0:n_cols, 0:1]),
                 reads=[("pb", bank)], writes=[dst_key])

        def ln_chunk(ko, b4, b5, precast):
            b = ko % 2
            if not precast:
                P.op("act", "act",
                     lambda e, ko=ko: e.copy(xb[:, ko, :], x_sb[:, ko, :]),
                     reads=[("x", ko)], writes=[("xb", ko)])
            P.op("act", "act",
                 lambda e, ko=ko, b=b: e.square(xsq[:, b, :], x_sb[:, ko, :]),
                 reads=[("x", ko)], writes=[("xsq", b)])
            P.op("pe", "pe",
                 lambda e, ko=ko: e.matmul(pb[b4][0:1, :], ones_bb[:, :],
                                           xb[:, ko, :],
                                           start=(ko == 0), stop=(ko == NCH - 1),
                                           skip_group_check=True),
                 reads=[("xb", ko), ("ones_bb",)], writes=[("pb", b4)])
            P.op("pe", "pe",
                 lambda e, ko=ko, b=b: e.matmul(pb[b5][0:1, :], ones_bb[:, :],
                                                xsq[:, b, :],
                                                start=(ko == 0), stop=(ko == NCH - 1),
                                                skip_group_check=True),
                 reads=[("xsq", b), ("ones_bb",)], writes=[("pb", b5)])

        def ln_rows(b4, b5):
            P.op("act", "act", lambda e: e.copy(row("s1"), pb[b4][0:1, :]),
                 reads=[("pb", b4)], writes=[rk("s1")])
            P.op("act", "act", lambda e: e.copy(row("s2"), pb[b5][0:1, :]),
                 reads=[("pb", b5)], writes=[rk("s2")])
            c = 1.0 / 4096.0
            P.op("dve", "dve", lambda e: e.tensor_scalar_mul(row("s1"), row("s1"), c),
                 reads=[rk("s1")], writes=[rk("s1")])
            P.op("dve", "dve", lambda e: e.tensor_scalar_mul(row("s2"), row("s2"), c),
                 reads=[rk("s2")], writes=[rk("s2")])
            P.op("dve", "dve",
                 lambda e: e.tensor_scalar_mul(negm_sb[0:1, :], row("s1"), -1.0),
                 reads=[rk("s1")], writes=[("negm",)])
            P.op("dve", "dve", lambda e: e.tensor_mul(row("t"), row("s1"), row("s1")),
                 reads=[rk("s1")], writes=[rk("t")])
            P.op("dve", "dve", lambda e: e.tensor_sub(row("t"), row("s2"), row("t")),
                 reads=[rk("s2"), rk("t")], writes=[rk("t")])
            P.op("dve", "dve",
                 lambda e: e.tensor_scalar_add(row("t"), row("t"), 1e-5),
                 reads=[rk("t")], writes=[rk("t")])
            P.op("act", "act",
                 lambda e: e.sqrt(row("r"), row("t")),
                 reads=[rk("t")], writes=[rk("r")])
            P.op("dve", "dve", lambda e: e.reciprocal(row("r"), row("r")),
                 reads=[rk("r")], writes=[rk("r")])

        def bcast_rsb():
            P.op("pe", "pe",
                 lambda e: e.matmul(pb[3][:, :], ones_sb[0:1, :], row("r"), start=True, stop=True),
                 reads=[rk("r"), ("ones",)], writes=[("pb", 3)])
            P.op("act", "act", lambda e: e.copy(rsb[:, :], pb[3][:, :]),
                 reads=[("pb", 3)], writes=[("rsb",)])
            P.op("pe", "pe",
                 lambda e: e.matmul(pb[2][:, :], ones_br[:, :], negm_sb[0:1, :],
                                    start=True, stop=True),
                 reads=[("negm",), ("ones_br",)], writes=[("pb", 2)])
            P.op("act", "act", lambda e: e.copy(negm_b[:, :], pb[2][:, :]),
                 reads=[("pb", 2)], writes=[("negm_b",)])

        def q_phase(p):
            for mo in range(QCH):
                rot = stream_block(wq_img, wqb, "wqb", mo, NCH, p)
                bank = mo % 2
                if p == 0:
                    wsum_calc(lambda ko, rot=rot: wqb[:, rot, ko, :], NCH, 128,
                              [("wqb", rot)],
                              wsumq_c[:, mo:mo + 1], ("wsumq", mo), 5)
                mm_sub(pb[bank][:, :], bank,
                       lambda ko, rot=rot: wqb[:, rot, ko, :],
                       lambda ko: xb[:, ko, :],
                       NCH, [("wqb", rot)], lambda ko: ("xb", ko))
                if mo == 0:
                    bcast_rsb()
                P.op("dve", "dve",
                     lambda e, bank=bank, mo=mo: e.scalar_tensor_tensor(
                         tmp0[:, :], negm_b[:, :], wsumq_c[:, mo:mo + 1], pb[bank][:, :],
                         ALU.mult, ALU.add),
                     reads=[("negm_b",), ("wsumq", mo), ("pb", bank)], writes=[("tmp0",)])
                P.op("dve", "dve",
                     lambda e: e.tensor_mul(tmp0[:, :], tmp0[:, :], rsb[:, :]),
                     reads=[("rsb",), ("tmp0",)], writes=[("tmp0",)])
                P.op("act", "act",
                     lambda e, mo=mo: e.activation(
                         tmp1[:, :], tmp0[:, :], AF.Relu, bias=bq_sb[:, mo:mo + 1], scale=1.0),
                     reads=[("tmp0",), ("bq",)], writes=[("tmp1",)])
                P.op("dve", "dve",
                     lambda e, mo=mo: e.scalar_tensor_tensor(
                         tmp0[:, :], tmp0[:, :], bq_sb[:, mo:mo + 1], tmp1[:, :],
                         ALU.add, ALU.subtract),
                     reads=[("tmp0",), ("bq",), ("tmp1",)], writes=[("tmp0",)])
                P.op("act", "act",
                     lambda e: e.activation(tmp0[:, :], tmp0[:, :], AF.Exp),
                     reads=[("tmp0",)], writes=[("tmp0",)])
                P.op("dve", "dve",
                     lambda e, mo=mo: e.tensor_add(sig[:, mo, :], tmp1[:, :], tmp0[:, :]),
                     reads=[("tmp0",), ("tmp1",)], writes=[("sig", mo)])
                if mo >= 1:
                    attn_step(mo - 1)
            attn_step(QCH - 1)
            for g in (QCH - 2, QCH - 1):
                attn_bcast(g)
                attn_retr(g)

        def rowi(slot, g):
            s = (2 * (g % 2) + slot) * TG
            return rows[0:1, s:s + TG]

        def atmp(g):
            return xsq[:, g % 2, :]

        def atmpk(g):
            return ("xsq", g % 2)

        def attn_bcast(g):
            rb = 2 + g % 2
            P.op("pe", "pe",
                 lambda e, g=g, rb=rb: e.matmul(pb[rb][:, :], ones_sb[0:1, :],
                                                rowi(1, g), start=True, stop=True),
                 reads=[("rows", 2 * (g % 2) + 1), ("ones",)], writes=[("pb", rb)])

        def attn_retr(gg):
            P.op("dve", "dve",
                 lambda e, gg=gg: e.tensor_tensor(
                     retr[:, gg, :], atmp(gg),
                     pb[2 + gg % 2][:, :], ALU.mult),
                 reads=[atmpk(gg), ("pb", 2 + gg % 2)],
                 writes=[("retr", gg)])

        def attn_step(g):
            nb = 6 + g % 2
            db = 4 + g % 2
            P.op("pe", "pe",
                 lambda e, g=g, nb=nb: e.matmul(pb[nb][:, :], m_b[:, g, :], sig[:, g, :],
                                                start=True, stop=True),
                 reads=[("m_b",), ("sig", g)], writes=[("pb", nb)])
            P.op("pe", "pe",
                 lambda e, g=g, db=db: e.matmul(pb[db][0:1, :], z_b[:, g:g + 1], sig[:, g, :],
                                                start=True, stop=True),
                 reads=[("z_b",), ("sig", g)], writes=[("pb", db)])
            if g >= 2:
                attn_bcast(g - 2)
            P.op("dve", "dve",
                 lambda e, g=g, db=db: e.tensor_single_scalar(rowi(0, g), pb[db][0:1, :],
                                                              1e-6, ALU.add),
                 reads=[("pb", db)], writes=[("rows", 2 * (g % 2))])
            P.op("dve", "dve",
                 lambda e, g=g: e.reciprocal(rowi(1, g), rowi(0, g)),
                 reads=[("rows", 2 * (g % 2))], writes=[("rows", 2 * (g % 2) + 1)])
            if g >= 2:
                attn_retr(g - 2)
            P.op("act", "act",
                 lambda e, g=g, nb=nb: e.copy(atmp(g), pb[nb][:, :]),
                 reads=[("pb", nb)], writes=[atmpk(g)])

        def stream_block_wo2(ob):
            rot = ob % 2
            P.op("gp", "qbw" + str(rot),
                 lambda e, rot=rot, ob=ob: e.dma_start(
                     bwb[:, rot, :, :], wo2_img[ob, :, :, :]),
                 writes=[("bwb", rot)])
            return rot

        def ao_phase(p):
            woe_r = woe_s.ap()
            for ob in range(NCH):
                bank = (0, 1, 4)[ob % 3]
                if p == 0:
                    rot = stream_block_wo2(ob)
                    mm_sub(pb[bank][:, :], bank,
                           lambda ko, rot=rot: bwb[:, rot, ko, :],
                           lambda ko: retr[:, ko // 4, :],
                           NCH, [("bwb", rot)],
                           lambda ko: ("retr", ko // 4))
                else:
                    prot = (ob // 4) % 2
                    if ob % 4 == 0:
                        P.op("sp", "qwo%d" % prot,
                             lambda e, ob=ob, prot=prot: e.dma_start(
                                 bwb[:, prot, :, :].rearrange("p (o g) m -> p o g m", o=4),
                                 woe_s[ob:ob + 4, :, :, :].rearrange("o k g m -> k o g m")),
                             reads=[("woe_s", ob + i, g) for i in range(4) for g in range(QCH)],
                             writes=[("bwb", prot)])
                    half = QCH * (ob % 4)
                    P.op("pe", "pe",
                         lambda e, prot=prot, bank=bank, half=half: [
                             e.matmul(pb[bank][:, :], bwb[:, prot, half + g, :], retr[:, g, :],
                                      start=(g == 0), stop=(g == QCH - 1))
                             for g in range(QCH)][-1],
                         reads=[("bwb", prot)] + [("retr", g) for g in range(QCH)],
                         writes=[("pb", bank)])
                P.op("dve", "dve",
                     lambda e, ob=ob, bank=bank: e.tensor_tensor(
                         x_sb[:, ob, :], x_sb[:, ob, :], pb[bank][:, :], ALU.add),
                     reads=[("x", ob), ("pb", bank)], writes=[("x", ob)])
                warm(5)
                if ob >= 2:
                    ln_chunk(ob - 2, 6, 7, precast=False)
                if p == 0:
                    rot = ob % 2
                    wv = bBb[:, :, 0:QCH, :]
                    P.op("dve", "dve",
                         lambda e, rot=rot, wv=wv: e.tensor_add(
                             wv[:, 0, :, :], bwb[:, rot, 0:32:4, :], bwb[:, rot, 1:32:4, :]),
                         reads=[("bwb", rot)], writes=[("bBb", 0)])
                    P.op("dve", "dve",
                         lambda e, rot=rot, wv=wv: e.tensor_add(
                             wv[:, 1, :, :], bwb[:, rot, 2:32:4, :], bwb[:, rot, 3:32:4, :]),
                         reads=[("bwb", rot)], writes=[("bBb", 1)])
                    P.op("dve", "dve",
                         lambda e, wv=wv: e.tensor_tensor(
                             wv[:, 0, :, :], wv[:, 0, :, :], wv[:, 1, :, :], ALU.add),
                         reads=[("bBb", 0), ("bBb", 1)], writes=[("bBb", 0)])
                    P.op("sp", "qps%d" % (ob % 2),
                         lambda e, ob=ob, wv=wv: e.dma_start(
                             woe_r[ob, :, :, :], wv[:, 0, :, :]),
                         reads=[("bBb", 0)],
                         writes=[("woe_s", ob, g) for g in range(QCH)])
            ln_chunk(NCH - 2, 6, 7, precast=False)
            ln_chunk(NCH - 1, 6, 7, precast=False)
            ln_rows(6, 7)

        def ffn_matmuls(p):
            if p == 0:
                wsum_calc(lambda ko: at_b[:, ko, :], NCH, 64, [("at_b",)],
                          wsumar_c[0:64, 0:1], ("wsumA",), 6)
                wsum_calc(lambda ko: rw_b[:, ko, :], NCH, 16, [("rw_b",)],
                          wsumar_c[0:16, 1:2], ("wsumR",), 7)
            mm_sub(pb[5][0:64, :], 5,
                   lambda ko: at_b[:, ko, :], lambda ko: xb[:, ko, :],
                   NCH, [("at_b",)], lambda ko: ("xb", ko))
            bcast_rsb()
            P.op("dve", "dve",
                 lambda e: e.scalar_tensor_tensor(
                     tmp0[0:64, :], negm_b[0:64, :], wsumar_c[0:64, 0:1], pb[5][0:64, :],
                     ALU.mult, ALU.add),
                 reads=[("negm_b",), ("wsumA",), ("pb", 5)], writes=[("tmp0",)])
            P.op("dve", "dve",
                 lambda e: e.tensor_tensor(res_rep[0:64, :], rsb[0:64, :], tmp0[0:64, :], ALU.mult),
                 reads=[("rsb",), ("tmp0",)], writes=[("res_rep",)])
            P.op("act", "act", lambda e: e.copy(res_rep[64:128, :], res_rep[0:64, :]),
                 reads=[("res_rep",)], writes=[("res_rep",)])
            mm_sub(pb[4][0:16, :], 4,
                   lambda ko: rw_b[:, ko, :], lambda ko: xb[:, ko, :],
                   NCH, [("rw_b",)], lambda ko: ("xb", ko))
            P.op("dve", "dve",
                 lambda e: e.scalar_tensor_tensor(
                     tmp0[0:16, :], negm_b[0:16, :], wsumar_c[0:16, 1:2], pb[4][0:16, :],
                     ALU.mult, ALU.add),
                 reads=[("negm_b",), ("wsumR",), ("pb", 4)], writes=[("tmp0",)])
            P.op("dve", "dve",
                 lambda e: e.tensor_tensor(lg_sb[:, :], rsb[0:16, :], tmp0[0:16, :], ALU.mult),
                 reads=[("rsb",), ("tmp0",)], writes=[("lg",)])

        def router_fwd_transposes(p):
            for j in range(4):
                bank = 6 + j % 2
                P.op("pe", "pe",
                     lambda e, j=j, bank=bank: e.transpose(
                         pb[bank][:, 0:16], lg_sb[:, 128 * j:128 * (j + 1)], eye_sb[0:16, 0:16]),
                     reads=[("lg",), ("eye",)], writes=[("pb", bank)])
                P.op("act", "act",
                     lambda e, j=j, bank=bank: e.copy(lgt_sb[:, 16 * j:16 * (j + 1)], pb[bank][:, 0:16]),
                     reads=[("pb", bank)], writes=[("lgt", j)])

        def router_topk(p):
            def cl(j, i):
                return col_sb[:, 8 * j + i:8 * j + i + 1]

            def sl(buf, j):
                return buf[:, 16 * j:16 * (j + 1)]

            J = range(4)
            for j in J:
                P.op("dve", "dve", lambda e, j=j: e.reduce_max(cl(j, 0), sl(lgt_sb, j), axis=AX.X),
                     reads=[("lgt", j)], writes=[("col", j, 0)])
            for j in J:
                P.op("dve", "dve", lambda e, j=j: e.tensor_scalar_mul(cl(j, 1), cl(j, 0), -1.0),
                     reads=[("col", j, 0)], writes=[("col", j, 1)])
                P.op("dve", "dve",
                     lambda e, j=j: e.tensor_single_scalar(sl(msk_sb, j), sl(lgt_sb, j), cl(j, 0), ALU.is_ge),
                     reads=[("lgt", j), ("col", j, 0)], writes=[("msk", j)])
            for j in J:
                P.op("dve", "dve",
                     lambda e, j=j: e.scalar_tensor_tensor(
                         sl(lg2_sb, j), sl(msk_sb, j), -1e30, sl(lgt_sb, j), ALU.mult, ALU.add),
                     reads=[("msk", j), ("lgt", j)], writes=[("lg2", j)])
                P.op("act", "act",
                     lambda e, j=j: e.activation(sl(e_sb, j), sl(lgt_sb, j), AF.Exp, bias=cl(j, 1), scale=1.0),
                     reads=[("lgt", j), ("col", j, 1)], writes=[("e_sb", j)])
            for j in J:
                P.op("dve", "dve", lambda e, j=j: e.reduce_max(cl(j, 2), sl(lg2_sb, j), axis=AX.X),
                     reads=[("lg2", j)], writes=[("col", j, 2)])
                P.op("dve", "dve", lambda e, j=j: e.reduce_sum(cl(j, 3), sl(e_sb, j), axis=AX.X),
                     reads=[("e_sb", j)], writes=[("col", j, 3)])
            for j in J:
                P.op("dve", "dve",
                     lambda e, j=j: e.tensor_single_scalar(sl(lg2_sb, j), sl(lg2_sb, j), cl(j, 2), ALU.is_ge),
                     reads=[("lg2", j), ("col", j, 2)], writes=[("lg2", j)])
                P.op("dve", "dve", lambda e, j=j: e.reciprocal(cl(j, 4), cl(j, 3)),
                     reads=[("col", j, 3)], writes=[("col", j, 4)])
            for j in J:
                P.op("dve", "dve",
                     lambda e, j=j: e.tensor_add(sl(msk_sb, j), sl(msk_sb, j), sl(lg2_sb, j)),
                     reads=[("msk", j), ("lg2", j)], writes=[("msk", j)])
                P.op("dve", "dve",
                     lambda e, j=j: e.tensor_single_scalar(sl(p_sb, j), sl(e_sb, j), cl(j, 4), ALU.mult),
                     reads=[("e_sb", j), ("col", j, 4)], writes=[("p_sb", j)])
            for j in J:
                P.op("dve", "dve",
                     lambda e, j=j: e.tensor_mul(sl(p_sb, j), sl(p_sb, j), sl(msk_sb, j)),
                     reads=[("p_sb", j), ("msk", j)], writes=[("p_sb", j)])
            for j in J:
                P.op("dve", "dve", lambda e, j=j: e.reduce_sum(cl(j, 5), sl(p_sb, j), axis=AX.X),
                     reads=[("p_sb", j)], writes=[("col", j, 5)])
            for j in J:
                P.op("dve", "dve",
                     lambda e, j=j: e.tensor_scalar(cl(j, 6), cl(j, 5), 2.0, 2e-6, ALU.mult, ALU.add),
                     reads=[("col", j, 5)], writes=[("col", j, 6)])
            for j in J:
                P.op("dve", "dve", lambda e, j=j: e.reciprocal(cl(j, 6), cl(j, 6)),
                     reads=[("col", j, 6)], writes=[("col", j, 6)])
            for j in J:
                P.op("dve", "dve",
                     lambda e, j=j: e.tensor_single_scalar(
                         sl(wt_sb, j), sl(p_sb, j), cl(j, 6), ALU.mult),
                     reads=[("p_sb", j), ("col", j, 6)], writes=[("wt", j)])

        def router_back(p):
            for j in range(4):
                bank = 6 + j % 2
                P.op("pe", "pe",
                     lambda e, j=j, bank=bank: e.transpose(
                         pb[bank][0:16, 0:128], wt_sb[:, 16 * j:16 * (j + 1)], eye_sb[:, :]),
                     reads=[("wt", j), ("eye",)], writes=[("pb", bank)])
                P.op("act", "act",
                     lambda e, j=j, bank=bank: e.copy(comb_sb[:, 128 * j:128 * (j + 1)], pb[bank][0:16, 0:128]),
                     reads=[("pb", bank)], writes=[("comb",)])

        def wh_phase(p):
            for g8 in range(QCH):
                bank = 6 + g8 % 2
                P.op("pe", "pe",
                     lambda e, g8=g8, bank=bank: e.matmul(
                         pb[bank][:, :], esel_b[:, g8, :], comb_sb[:, :], start=True, stop=True),
                     reads=[("esel_b",), ("comb",)], writes=[("pb", bank)])
                P.op("dve", "dve",
                     lambda e, g8=g8, bank=bank: e.tensor_tensor(
                         retr[:, g8, :], res_rep[:, :], pb[bank][:, :], ALU.mult),
                     reads=[("res_rep",), ("pb", bank)], writes=[("retr", g8)])

        def out_base(p, ob):
            rotw = stream_block(base_img, bwb, "bwb", ob, NCH, p)
            bank = (0, 1, 4)[ob % 3]
            if p == 0:
                wsum_calc(lambda ko, rotw=rotw: bwb[:, rotw, ko, :], NCH, 128,
                          [("bwb", rotw)],
                          wsumb_c[:, ob:ob + 1], ("wsumb", ob), 2 + ob % 2)
            mm_sub(pb[bank][:, :], bank,
                   lambda ko, rotw=rotw: bwb[:, rotw, ko, :],
                   lambda ko: xb[:, ko, :],
                   NCH, [("bwb", rotw)], lambda ko: ("xb", ko))
            warm(5)
            tb, tk = (tmp0, ("tmp0",)) if ob % 2 == 0 else (tmp1, ("tmp1",))
            P.op("dve", "dve",
                 lambda e, ob=ob, bank=bank, tb=tb: e.scalar_tensor_tensor(
                     tb[:, :], negm_b[:, :], wsumb_c[:, ob:ob + 1], pb[bank][:, :],
                     ALU.mult, ALU.add),
                 reads=[("negm_b",), ("wsumb", ob), ("pb", bank)], writes=[tk])
            P.op("dve", "dve",
                 lambda e, tb=tb: e.tensor_mul(tb[:, :], tb[:, :], rsb[:, :]),
                 reads=[("rsb",), tk], writes=[tk])
            P.op("dve", "dve",
                 lambda e, ob=ob, tb=tb: e.scalar_tensor_tensor(
                     x_sb[:, ob, :], tb[:, :], bb_sb[:, ob:ob + 1], x_sb[:, ob, :],
                     ALU.add, ALU.add),
                 reads=[tk, ("bb",), ("x", ob)], writes=[("x", ob)])

        def out_delta(p, ob):
            prot = (ob // 2) % 4
            if ob % 2 == 0:
                if p == 0:
                    P.op("gp", "qbB%d" % prot,
                         lambda e, ob=ob, prot=prot: e.dma_start(
                             bBb[:, prot, :, :].rearrange("p (o g) m -> p o g m", o=2),
                             bB_img[ob:ob + 2, :, :, :].rearrange("o k g m -> k o g m")),
                         writes=[("bBb", prot)])
                    P.op("sp", "qsc%d" % (prot % 2),
                         lambda e, ob=ob, prot=prot: e.dma_start(
                             bB_s[ob:ob + 2, :, :, :].rearrange("o k g m -> k o g m"),
                             bBb[:, prot, :, :].rearrange("p (o g) m -> p o g m", o=2)),
                         reads=[("bBb", prot)],
                         writes=[("bBb_s", ob)])
                else:
                    P.op("sp", "hbB%d" % prot,
                         lambda e, ob=ob, prot=prot: e.dma_start(
                             bBb[:, prot, :, :].rearrange("p (o g) m -> p o g m", o=2),
                             bB_s[ob:ob + 2, :, :, :].rearrange("o k g m -> k o g m")),
                         reads=[("bBb_s", ob)], writes=[("bBb", prot)])
            half = QCH * (ob % 2)
            dbank = 2 + ob % 2
            P.op("pe", "pe",
                 lambda e, prot=prot, dbank=dbank, half=half: [
                     e.matmul(pb[dbank][:, :], bBb[:, prot, half + g, :], retr[:, g, :],
                              start=(g == 0), stop=(g == QCH - 1))
                     for g in range(QCH)][-1],
                 reads=[("bBb", prot)] + [("retr", g) for g in range(QCH)],
                 writes=[("pb", dbank)])
            ro = ob % 3
            P.op("dve", "dve",
                 lambda e, ob=ob, ro=ro, dbank=dbank: e.tensor_tensor(
                     outb[:, ro, :], x_sb[:, ob, :], pb[dbank][:, :], ALU.add),
                 reads=[("x", ob), ("pb", dbank), ("outb", ro)], writes=[("outb", ro)])
            P.op("act", "qo%d" % ro,
                 lambda e, p=p, ob=ob, ro=ro: e.dma_start(out_img[p, ob, :, :], outb[:, ro, :]),
                 reads=[("outb", ro)])

        for p in range(npass):
            if p == 0:
                for c in range(8):
                    P.op("act", "qx%d" % (c % 4),
                         lambda e, c=c: e.dma_start(
                             x_sb[:, 4 * c:4 * (c + 1), :], x_img[0, :, 4 * c:4 * (c + 1), :]),
                         writes=[("x", ko) for ko in range(4 * c, 4 * (c + 1))])
            if p == 0:
                for ko in range(NCH):
                    ln_chunk(ko, 4, 5, precast=False)
                ln_rows(4, 5)
            q_phase(p)
            ao_phase(p)
            ffn_matmuls(p)
            out_base(p, 0)
            out_base(p, 1)
            router_fwd_transposes(p)
            for ob in range(2, 10):
                out_base(p, ob)
            router_topk(p)
            for ob in range(10, 14):
                out_base(p, ob)
            router_back(p)
            wh_phase(p)
            for ob in range(14, NCH):
                out_base(p, ob)
            for ob in range(NCH):
                out_delta(p, ob)
                warm(0)
                if p + 1 < npass and ob % 4 == 3:
                    c = ob // 4
                    P.op("act", "qx%d" % (c % 4),
                         lambda e, p=p, c=c: e.dma_start(
                             x_sb[:, 4 * c:4 * (c + 1), :], x_img[p + 1, :, 4 * c:4 * (c + 1), :]),
                         writes=[("x", ko) for ko in range(4 * c, 4 * (c + 1))])
                    if c >= 2:
                        for ko in range(4 * (c - 2), 4 * (c - 1)):
                            P.op("act", "act",
                                 lambda e, ko=ko: e.copy(xb[:, ko, :], x_sb[:, ko, :]),
                                 reads=[("x", ko)], writes=[("xb", ko)])
                            ln_chunk(ko, 4, 5, precast=True)
            if p + 1 < npass:
                for ko in range(24, 32):
                    P.op("act", "act",
                         lambda e, ko=ko: e.copy(xb[:, ko, :], x_sb[:, ko, :]),
                         reads=[("x", ko)], writes=[("xb", ko)])
                    ln_chunk(ko, 4, 5, precast=True)
                    warm(0)
                ln_rows(4, 5)

        P.wait("sp", "qo0", P.cnt["qo0"])
        P.wait("sp", "qo1", P.cnt["qo1"])

        with nc.Block() as block:
            @block.tensor
            def _(t):
                P.run_stream("pe", t, sems)

            @block.scalar
            def _(s):
                P.run_stream("act", s, sems)

            @block.vector
            def _(v):
                P.run_stream("dve", v, sems)

            @block.gpsimd
            def _(g):
                P.run_stream("gp", g, sems)

            @block.sync
            def _(s):
                P.run_stream("sp", s, sems)

    return nc


def stage_inputs(inputs, npass, tokens_per_core, ncores):
    x = np.ascontiguousarray(np.asarray(inputs["L0_attn_hidden_states"], np.float32).reshape(-1, 4096))
    Wq = np.asarray(inputs["L0_attn_Wq"], np.float32)
    bq = np.asarray(inputs["L0_attn_bq"], np.float32)
    Wo = np.asarray(inputs["L0_attn_Wo"], np.float32)
    M = np.asarray(inputs["L0_attn_M"], np.float32)
    z = np.asarray(inputs["L0_attn_z"], np.float32)
    base_w = np.asarray(inputs["L0_ffn_base_w"], np.float32)
    base_b = np.asarray(inputs["L0_ffn_base_b"], np.float32)
    A = np.asarray(inputs["L0_ffn_A"], np.float32)
    B = np.asarray(inputs["L0_ffn_B"], np.float32)
    router_w = np.asarray(inputs["L0_ffn_router_w"], np.float32)

    sel = np.concatenate([np.arange(512 * g, 512 * g + 128) for g in range(8)])
    WqselT = np.ascontiguousarray(Wq.T[:, sel])
    wq_img = np.ascontiguousarray(
        WqselT.reshape(NCH, 128, QCH, 128).transpose(2, 1, 0, 3))
    base_img = np.ascontiguousarray(
        np.ascontiguousarray(base_w.T).reshape(NCH, 128, NCH, 128).transpose(2, 1, 0, 3))
    B_flatT = np.ascontiguousarray(np.transpose(B, (0, 2, 1)).reshape(1024, 4096))
    bB_img = np.ascontiguousarray(
        B_flatT.reshape(QCH, 128, NCH, 128).transpose(2, 1, 0, 3))
    wo2_img = np.ascontiguousarray(
        np.ascontiguousarray(Wo.T).reshape(NCH, 128, NCH, 128).transpose(2, 1, 0, 3))
    at_img = np.ascontiguousarray(A.T.reshape(NCH, 128, 64).transpose(1, 0, 2))
    rw_img = np.ascontiguousarray(router_w.T.reshape(NCH, 128, 16).transpose(1, 0, 2))
    m_img = np.ascontiguousarray(M.transpose(1, 0, 2))
    z_img = np.ascontiguousarray(z.T)
    bq_img = np.ascontiguousarray(bq[sel].reshape(QCH, 128).T)
    bb_img = np.ascontiguousarray(base_b.reshape(NCH, 128).T)
    esel = np.zeros((16, QCH, 128), np.float32)
    for g8 in range(QCH):
        for m in range(128):
            esel[2 * g8 + m // 64, g8, m] = 1.0
    eye_img = np.eye(128, dtype=np.float32)

    shared = dict(wq_img=wq_img, base_img=base_img, bB_img=bB_img, wo2_img=wo2_img,
                  at_img=at_img, rw_img=rw_img, m_img=m_img, z_img=z_img,
                  bq_img=bq_img, bb_img=bb_img, esel_img=esel, eye_img=eye_img)
    in_maps = []
    for c in range(ncores):
        xc = x[c * tokens_per_core:(c + 1) * tokens_per_core]
        x_img = np.ascontiguousarray(
            xc.T.reshape(NCH, 128, npass, TG).transpose(2, 1, 0, 3))
        in_maps.append(dict(shared, x_img=x_img))
    return in_maps


def unstage_output(results, npass, tokens_per_core, ncores, out_shape):
    outs = []
    for c in range(ncores):
        oi = np.asarray(results[c]["out_img"], np.float32)
        xc = oi.transpose(1, 2, 0, 3).reshape(4096, tokens_per_core).T
        outs.append(xc)
    return np.concatenate(outs, axis=0).reshape(out_shape)


def kernel(L0_attn_hidden_states, L0_attn_Wq, L0_attn_bq, L0_attn_Wk, L0_attn_bk,
           L0_attn_Wv, L0_attn_bv, L0_attn_Wo, L0_attn_M, L0_attn_z,
           L0_ffn_base_w, L0_ffn_base_b, L0_ffn_A, L0_ffn_B, L0_ffn_router_w):
    _ = (L0_attn_Wk, L0_attn_bk, L0_attn_Wv, L0_attn_bv)
    inputs = dict(
        L0_attn_hidden_states=L0_attn_hidden_states, L0_attn_Wq=L0_attn_Wq,
        L0_attn_bq=L0_attn_bq, L0_attn_Wo=L0_attn_Wo, L0_attn_M=L0_attn_M,
        L0_attn_z=L0_attn_z, L0_ffn_base_w=L0_ffn_base_w,
        L0_ffn_base_b=L0_ffn_base_b, L0_ffn_A=L0_ffn_A, L0_ffn_B=L0_ffn_B,
        L0_ffn_router_w=L0_ffn_router_w)
    npass = NPASS_FULL
    tokens_per_core = npass * TG
    nc = build_nc(npass)
    in_maps = stage_inputs(inputs, npass, tokens_per_core, NCORES)
    res = run_bass_kernel_spmd(nc, in_maps, core_ids=list(range(NCORES)))
    out_shape = np.asarray(L0_attn_hidden_states).shape
    return unstage_output(res.results, npass, tokens_per_core, NCORES, out_shape)
